# version 03 (final)
# baseline (speedup 1.0000x reference)
import numpy as np
import concourse.bass as bass
import concourse.mybir as mybir
from concourse.bass_utils import run_bass_kernel_spmd
from concourse.aot_env import allow_ici_all_to_all

F32 = mybir.dt.float32
BF16 = mybir.dt.bfloat16
AF = mybir.ActivationFunctionType
OP = mybir.AluOpType
AX = mybir.AxisListType.X

NCORES = 8
L = 16384
TOK = 2048
D = 1024
EE = 2048
N = 16
DTR = 128
NEXP = 8
ET = 16
DT = 8
HT = 8
TT = 16
TG = 512
XN_W = TOK + 3

XN0 = 0
BBT0, CBT0 = 0, 8192
XC0 = 8 * XN_W
MOE0 = XC0
YXT0 = XC0 + 16384
AB = XC0 + ET * XN_W

WB_XR = 8192
W1_0, W2_0 = 0, 8192
P7_DLT, P7_U, P7_DU, P7_W = 0, 2048, 4096, 6144

G_HID, G_RELU, G_SQ, G_GATE = 0, 4096, 4608, 5120
B7_D, B7_U, B7_XR, B7_DU, B7_W = 0, 512, 1024, 1536, 2048

SQ0 = 0
CACC = 0
SP0, SP1 = 0, 512
DA0, DA1, HF = 0, 2048, 4096
DA70, DA71, HF7, Y7, HC7, OUT7 = 0, 512, 1024, 1536, 2048, 2560


STOP = None
P1I = 0


def build_program():
    nc = bass.Bass("TRN2", target_bir_lowering=False, detect_race_conditions=False)

    x_in = nc.dram_tensor("x_in", [TOK + 3, D], F32, kind="ExternalInput").ap()
    W_in = nc.dram_tensor("W_in", [D, 2 * EE], F32, kind="ExternalInput").ap()
    b_in = nc.dram_tensor("b_in", [1, 2 * EE], F32, kind="ExternalInput").ap()
    conv_w = nc.dram_tensor("conv_w", [EE, 4], F32, kind="ExternalInput").ap()
    conv_b = nc.dram_tensor("conv_b", [1, EE], F32, kind="ExternalInput").ap()
    xproj_w = nc.dram_tensor("xproj_w", [EE, 160], F32, kind="ExternalInput").ap()
    dtproj_w = nc.dram_tensor("dtproj_w", [DTR, EE], F32, kind="ExternalInput").ap()
    dt_bias = nc.dram_tensor("dt_bias", [1, EE], F32, kind="ExternalInput").ap()
    A_log = nc.dram_tensor("A_log", [EE, N], F32, kind="ExternalInput").ap()
    D_par = nc.dram_tensor("D_par", [1, EE], F32, kind="ExternalInput").ap()
    W_out = nc.dram_tensor("W_out", [EE, D], F32, kind="ExternalInput").ap()
    b_out = nc.dram_tensor("b_out", [1, D], F32, kind="ExternalInput").ap()
    gate_wT = nc.dram_tensor("gate_wT", [D, NEXP], F32, kind="ExternalInput").ap()
    lb_bias = nc.dram_tensor("lb_bias", [1, NEXP], F32, kind="ExternalInput").ap()
    w1T = nc.dram_tensor("w1T", [NEXP, D, D], F32, kind="ExternalInput").ap()
    w2T = nc.dram_tensor("w2T", [NEXP, D, D], F32, kind="ExternalInput").ap()
    cfcT = nc.dram_tensor("cfcT", [D, D], F32, kind="ExternalInput").ap()
    cprojT = nc.dram_tensor("cprojT", [D, D], F32, kind="ExternalInput").ap()
    halo_mask = nc.dram_tensor("halo_mask", [1, 1], F32, kind="ExternalInput").ap()
    core_mask = nc.dram_tensor("core_mask", [1, NCORES], F32, kind="ExternalInput").ap()
    ident_in = nc.dram_tensor("ident_in", [128, 128], F32, kind="ExternalInput").ap()

    out = nc.dram_tensor("out", [TOK, D], F32, kind="ExternalOutput").ap()

    d_xr = nc.dram_tensor("d_xr", [EE, TOK], BF16).ap()
    d_u = nc.dram_tensor("d_u", [EE, TOK], BF16).ap()
    d_delta = nc.dram_tensor("d_delta", [EE, TOK], BF16).ap()
    d_bc = nc.dram_tensor("d_bc", [32, TOK], BF16).ap()
    d_we = nc.dram_tensor("d_we", [NEXP, TOK], BF16).ap()
    d_exin = nc.dram_tensor("d_exin", [128, 512], F32).ap()
    d_exout = nc.dram_tensor("d_exout", [NCORES, 128, 512], F32).ap()

    ctx = []
    def sb(name, shape, dt):
        t = nc.sbuf_tensor(name, shape, dt)
        ctx.append(t)
        return t.__enter__()
    def psum(name, shape, dt):
        t = nc.psum_tensor(name, shape, dt)
        ctx.append(t)
        return t.__enter__()
    def sem(name):
        t = nc.semaphore(name)
        ctx.append(t)
        return t.__enter__()

    arena_b = sb("arena_b", [128, AB], BF16)
    arena_f = sb("arena_f", [128, 6144], F32)
    workb = sb("workb", [128, 16384], BF16)
    wb16 = sb("wb16", [128, 5200], BF16)
    wf32 = sb("wf32", [128, 1024], F32)
    dt_sb = sb("dt_sb", [128, TOK], BF16)
    bc_sb = sb("bc_sb", [32, TOK], BF16)
    Atile = sb("Atile", [128, ET * N], F32)
    biases = sb("biases", [128, 80], F32)
    convp = sb("convp", [128, 80], F32)
    identb = sb("identb", [128, 128], BF16)
    identf = sb("identf", [128, 128], F32)
    exch = sb("exch", [128, 512], F32)
    hin = sb("hin", [128, 256], F32)
    hcur = sb("hcur", [128, 256], F32)
    maskb = sb("maskb", [128, NCORES], F32)
    consts = sb("consts", [128, 8], F32)
    lbb = sb("lbb", [128, NEXP], F32)
    we_sb = sb("we_sb", [128, TT * NEXP], F32)
    weT = sb("weT", [8, TOK], BF16)
    web = sb("web", [128, TOK], BF16)
    stats = sb("stats", [128, 8], F32)

    ps0 = psum("ps0", [128, TG], F32)
    ps1 = psum("ps1", [128, TG], F32)
    ps2 = psum("ps2", [128, TG], F32)
    ps3 = psum("ps3", [128, TG], F32)
    pst = psum("pst", [128, 256], F32)
    pstb = psum("pstb", [128, 1024], BF16)
    psl = psum("psl", [128, NEXP], F32)

    SEMS = {nm: sem("m_" + nm)
            for nm in ('c', 'x', 'w', 'bc', 'ld', 'ex', 'g', 'cc', 'v', 's', 't')}

    cnt = {}

    def make_walk(me, eng):
        lastwait = {}
        cnt.clear()
        cnt.update({k: 0 for k in SEMS})

        def E(engn, fn, inc=1, waits=None, semn=None):
            semn = semn or engn
            if engn == me:
                if waits:
                    for s2, c2 in waits.items():
                        if c2 > 0 and lastwait.get(s2, -1) < c2:
                            eng.wait_ge(SEMS[s2], c2)
                            lastwait[s2] = c2
                r = fn(eng)
                if inc:
                    r.then_inc(SEMS[semn], inc)
            cnt[semn] += inc
            return cnt[semn]

        def dma(q, stream, out_ap, in_ap, waits=None):
            return E(q, lambda e: e.dma_start(out=out_ap, in_=in_ap), 16, waits,
                     semn=stream)

        try:
            walk(me, eng, E, dma)
        except Exception as ex:
            if type(ex).__name__ != '_Stop':
                raise
        if me == 'g':
            eng.wait_ge(SEMS['g'], cnt['g'])

    def walk(me, eng, E, dma):
        def mkw(base, *pairs):
            d = dict(base)
            for en, c in pairs:
                d[en] = max(d.get(en, 0), c)
            return d

        class _Stop(Exception):
            pass

        def ckpt(name):
            if STOP == name:
                for ti in range(TT):
                    dma('g', 'g', out[ti * 128:(ti + 1) * 128, :],
                        arena_f[:, 0:1024])
                raise _Stop()

        A_ = lambda c0, w: arena_f[:, c0:c0 + w]
        xn = lambda c, t0, w: arena_b[:, XN0 + c * XN_W + 3 + t0: XN0 + c * XN_W + 3 + t0 + w]
        xnh = lambda c: arena_b[:, XN0 + c * XN_W: XN0 + c * XN_W + 3]
        xcv = lambda i, t0, w: arena_b[:, XC0 + i * XN_W + 3 + t0: XC0 + i * XN_W + 3 + t0 + w]
        moe = lambda dd, t0, w: arena_b[:, MOE0 + dd * TOK + t0: MOE0 + dd * TOK + t0 + w]
        bb7a = lambda r: arena_b[:, XC0 + r * TOK: XC0 + (r + 1) * TOK]
        bbt = lambda r, w: arena_b[:, BBT0 + r * TG: BBT0 + r * TG + w]
        cbt = lambda r, w: arena_b[:, CBT0 + r * TG: CBT0 + r * TG + w]
        yxt = lambda k, w: arena_b[:, YXT0 + k * TG: YXT0 + k * TG + w]
        tgs = [(0, TG), (TG, TG), (2 * TG, TG), (3 * TG, TG)]

        dma('sy', 'c', identf[:, :], ident_in)
        dma('sy', 'c', consts[:, 0:1], halo_mask.to_broadcast([128, 1]))
        dma('sy', 'c', maskb[:, :], core_mask.to_broadcast([128, NCORES]))
        dma('sy', 'c', lbb[:, :], lb_bias.to_broadcast([128, NEXP]))
        for i in range(ET):
            dma('sy', 'c', Atile[:, i * N:(i + 1) * N], A_log[i * 128:(i + 1) * 128, :])
        for i in range(ET):
            dma('sy', 'c', biases[:, i:i + 1],
                b_in[0:1, i * 128:(i + 1) * 128].transpose([1, 0]))
            dma('sy', 'c', biases[:, 16 + i:17 + i],
                b_in[0:1, EE + i * 128:EE + (i + 1) * 128].transpose([1, 0]))
            dma('sy', 'c', biases[:, 32 + i:33 + i],
                dt_bias[0:1, i * 128:(i + 1) * 128].transpose([1, 0]))
            dma('sy', 'c', biases[:, 56 + i:57 + i],
                D_par[0:1, i * 128:(i + 1) * 128].transpose([1, 0]))
        for dd in range(DT):
            dma('sy', 'c', biases[:, 48 + dd:49 + dd],
                b_out[0:1, dd * 128:(dd + 1) * 128].transpose([1, 0]))
        for i in range(ET):
            dma('sy', 'c', convp[:, i * 5:i * 5 + 4], conv_w[i * 128:(i + 1) * 128, :])
            dma('sy', 'c', convp[:, i * 5 + 4:i * 5 + 5],
                conv_b[0:1, i * 128:(i + 1) * 128].transpose([1, 0]))
        c_p0 = cnt['c']
        E('v', lambda e: e.tensor_copy(out=identb[:, :], in_=identf[:, :]),
          waits={'c': c_p0})
        E('v', lambda e: e.memset(consts[:, 1:2], 1e-5))
        E('v', lambda e: e.memset(consts[:, 2:3], 0.0))
        c_z0 = cnt['v']
        E('s', lambda e: e.activation(out=Atile[:, :], in_=Atile[:, :], func=AF.Exp),
          waits={'c': c_p0})
        c_A = E('v', lambda e: e.tensor_scalar(out=Atile[:, :], in0=Atile[:, :],
                                               scalar1=-1.0, scalar2=None, op0=OP.mult),
                waits={'s': cnt['s']})
        ckpt('p0')

        c_pstb = [0, 0]
        for i in range(TT + 1):
            nrow = 128 if i < TT else 3
            src = x_in[3 + i * 128: 3 + i * 128 + nrow, :] if i < TT else x_in[0:3, :]
            cx = dma('sy', 'x', wf32[0:nrow, :], src,
                     waits={'v': cnt['v'], 's': cnt['s']} if i else None)
            E('v', lambda e, nr=nrow: e.reduce_sum(out=stats[0:nr, 0:1], in_=wf32[0:nr, :],
                                                   axis=AX, negate=True), waits={'x': cx})
            E('s', lambda e, nr=nrow: e.activation(out=A_(SQ0, 1024)[0:nr, :],
                                                   in_=wf32[0:nr, :], func=AF.Square),
              waits={'x': cx})
            E('v', lambda e, nr=nrow: e.reduce_sum(out=stats[0:nr, 1:2],
                                                   in_=A_(SQ0, 1024)[0:nr, :], axis=AX),
              waits={'s': cnt['s']})
            E('v', lambda e, nr=nrow: e.tensor_scalar(out=stats[0:nr, 0:1], in0=stats[0:nr, 0:1],
                                                      scalar1=1.0 / D, scalar2=None, op0=OP.mult),
              waits={'v': cnt['v']})
            E('v', lambda e, nr=nrow: e.tensor_scalar(out=stats[0:nr, 1:2], in0=stats[0:nr, 1:2],
                                                      scalar1=1.0 / D, scalar2=None, op0=OP.mult),
              waits={'v': cnt['v']})
            E('v', lambda e, nr=nrow: e.tensor_tensor(out=stats[0:nr, 2:3], in0=stats[0:nr, 0:1],
                                                      in1=stats[0:nr, 0:1], op=OP.mult),
              waits={'v': cnt['v']})
            E('v', lambda e, nr=nrow: e.tensor_tensor(out=stats[0:nr, 2:3], in0=stats[0:nr, 1:2],
                                                      in1=stats[0:nr, 2:3], op=OP.subtract),
              waits={'v': cnt['v']})
            c_v1 = E('s', lambda e, nr=nrow: e.activation(out=stats[0:nr, 3:4],
                                                          in_=stats[0:nr, 2:3], func=AF.Sqrt,
                                                          bias=consts[0:nr, 1:2]),
                     waits={'v': cnt['v'], 'c': c_p0})
            E('v', lambda e, nr=nrow: e.reciprocal(out=stats[0:nr, 3:4], in_=stats[0:nr, 3:4]),
              waits={'s': c_v1})
            E('v', lambda e, nr=nrow: e.tensor_tensor(out=stats[0:nr, 4:5], in0=stats[0:nr, 0:1],
                                                      in1=stats[0:nr, 3:4], op=OP.mult),
              waits={'v': cnt['v']})
            cv = E('v', lambda e, nr=nrow: e.tensor_scalar(out=workb[0:nr, 0:1024],
                                                           in0=wf32[0:nr, :],
                                                           scalar1=stats[0:nr, 3:4],
                                                           scalar2=stats[0:nr, 4:5],
                                                           op0=OP.mult, op1=OP.add),
                    waits={'v': cnt['v']})
            for c in range(DT):
                ct = E('t', lambda e, nr=nrow, cc=c: e.transpose(
                    out=pstb[0:128, cc * 128:cc * 128 + nr],
                    in_=workb[0:nr, cc * 128:(cc + 1) * 128],
                    identity=identb[0:nr, 0:nr]),
                    inc=(1 if c == DT - 1 else 0),
                    waits={'v': cv, 's': c_pstb[0]} if c == 0 else None)
            for c in range(DT):
                dst = (xn(c, i * 128, 128) if i < TT else xnh(c))
                c_pstb[0] = E('s', lambda e, d=dst, nr=nrow, cc=c: e.copy(
                    out=d, in_=pstb[0:128, cc * 128:cc * 128 + nr]),
                    waits={'t': ct} if c == 0 else None)
            if i == P1I:
                ckpt('p1a')
            if i == TT - 1:
                ckpt('p1b')
        c_xn_s = cnt['s']
        ckpt('p1')

        pscons = {'ps0': ('v', 0), 'ps1': ('v', 0), 'ps2': ('v', 0), 'ps3': ('v', 0)}
        psbuf = {'ps0': ps0, 'ps1': ps1, 'ps2': ps2, 'ps3': ps3}
        xr_spill = [0, 0, 0, 0]
        c_xr_prev = 0
        for m in range(32):
            cw = 0
            for k in range(DT):
                cw = dma('sy', 'w', wf32[:, k * 128:(k + 1) * 128],
                         W_in[k * 128:(k + 1) * 128, m * 128:(m + 1) * 128],
                         waits={'s': cnt['s'], 'v': cnt['v']} if k == 0 else None)
            cs = E('s', lambda e: e.copy(out=wb16[:, 0:1024], in_=wf32[:, :]),
                   waits={'w': cw})
            grps = tgs + [('h', 3)] if m < 16 else tgs
            for gi, (t0, w) in enumerate(grps):
                pn = 'ps0' if (gi % 2 == 0) else 'ps1'
                pb = psbuf[pn]
                for k in range(DT):
                    rhs = xn(k, t0, w) if t0 != 'h' else xnh(k)
                    fw = mkw({'s': max(cs, c_xn_s)}, pscons[pn]) if k == 0 else None
                    ct = E('t', lambda e, kk=k, r=rhs, p=pb, ww=w: e.matmul(
                        out=p[:, 0:ww], lhsT=wb16[:, kk * 128:(kk + 1) * 128], rhs=r,
                        start=(kk == 0), stop=(kk == DT - 1)),
                        inc=(1 if k == DT - 1 else 0), waits=fw)
                if m < 16:
                    if t0 == 'h':
                        cc = E('v', lambda e, p=pb, mm=m: e.tensor_scalar(
                            out=arena_b[:, XC0 + mm * XN_W: XC0 + mm * XN_W + 3],
                            in0=p[:, 0:3], scalar1=biases[:, mm:mm + 1],
                            scalar2=consts[:, 0:1], op0=OP.add, op1=OP.mult),
                            waits={'t': ct})
                        pscons[pn] = ('v', cc)
                    else:
                        cc = E('v', lambda e, p=pb, mm=m, tt0=t0, ww=w: e.tensor_scalar(
                            out=xcv(mm, tt0, ww), in0=p[:, 0:ww],
                            scalar1=biases[:, mm:mm + 1], scalar2=None, op0=OP.add),
                            waits={'t': ct})
                        pscons[pn] = ('v', cc)
                else:
                    slot = gi
                    cs2 = E('s', lambda e, p=pb, mm=m, ww=w, so=slot * TG: e.activation(
                        out=workb[:, WB_XR + so:WB_XR + so + ww], in_=p[:, 0:ww],
                        func=AF.Silu, bias=biases[:, mm:mm + 1]),
                        waits={'t': ct, 'g': c_xr_prev})
                    pscons[pn] = ('s', cs2)
                    xr_spill[slot] = dma('g', 'g',
                                         d_xr[(m - 16) * 128:(m - 15) * 128, t0:t0 + w],
                                         workb[:, WB_XR + slot * TG:WB_XR + slot * TG + w],
                                         waits={'s': cs2})
            if m >= 16:
                c_xr_prev = xr_spill[3]
        c_xc_v = cnt['v']
        ckpt('p3')

        cxp = 0
        for k in range(ET):
            cxp = dma('sy', 'w', wf32[:, 0:160], xproj_w[k * 128:(k + 1) * 128, :],
                      waits={'s': cnt['s'], 'v': cnt['v']} if k == 0 else {'s': cnt['s']})
            E('s', lambda e, kk=k: e.copy(out=wb16[:, kk * 160:(kk + 1) * 160],
                                          in_=wf32[:, 0:160]), waits={'w': cxp})
        c_xpw = cnt['s']
        c_xp_mm = 0
        c_usp_prev = c_xr_prev
        u_spill = [0] * ET
        for tgi, (t0, w) in enumerate(tgs):
            c_u = 0
            for i in range(ET):
                cv = E('v', lambda e, ii=i, tt0=t0, ww=w: e.tensor_scalar(
                    out=A_(CACC, TG)[:, 0:ww], in0=xcv(ii, tt0 - 3, ww),
                    scalar1=convp[:, ii * 5:ii * 5 + 1], scalar2=None, op0=OP.mult),
                    waits={'v': c_xc_v, 'c': c_p0, 's': cnt['s']})
                for k in range(1, 4):
                    cv = E('v', lambda e, ii=i, tt0=t0, ww=w, kk=k: e.scalar_tensor_tensor(
                        out=A_(CACC, TG)[:, 0:ww], in0=xcv(ii, tt0 - 3 + kk, ww),
                        scalar=convp[:, ii * 5 + kk:ii * 5 + kk + 1],
                        in1=A_(CACC, TG)[:, 0:ww], op0=OP.mult, op1=OP.add))
                c_u = E('s', lambda e, ii=i, ww=w: e.activation(
                    out=workb[:, ii * TG:ii * TG + ww], in_=A_(CACC, TG)[:, 0:ww],
                    func=AF.Silu, bias=convp[:, ii * 5 + 4:ii * 5 + 5]),
                    waits={'v': cv, 't': c_xp_mm, 'g': c_usp_prev})
                u_spill[i] = dma('g', 'g', d_u[i * 128:(i + 1) * 128, t0:t0 + w],
                                 workb[:, i * TG:i * TG + w], waits={'s': c_u})
            c_usp_prev = u_spill[ET - 1]
            for k in range(ET):
                fw2 = mkw({'s': max(c_u, c_xpw)}, pscons['ps2'], pscons['ps3']) \
                    if k == 0 else None
                E('t', lambda e, kk=k, ww=w: e.matmul(
                    out=ps2[:, 0:ww], lhsT=wb16[:, kk * 160:kk * 160 + 128],
                    rhs=workb[:, kk * TG:kk * TG + ww], start=(kk == 0),
                    stop=(kk == ET - 1)), inc=(1 if k == ET - 1 else 0), waits=fw2)
                c_xp_mm = E('t', lambda e, kk=k, ww=w: e.matmul(
                    out=ps3[0:32, 0:ww], lhsT=wb16[:, kk * 160 + 128:(kk + 1) * 160],
                    rhs=workb[:, kk * TG:kk * TG + ww], start=(kk == 0),
                    stop=(kk == ET - 1)), inc=(1 if k == ET - 1 else 0))
            cd = E('s', lambda e, tt0=t0, ww=w: e.copy(out=dt_sb[:, tt0:tt0 + ww],
                                                       in_=ps2[:, 0:ww]),
                   waits={'t': c_xp_mm})
            pscons['ps2'] = ('s', cd)
            cb2 = E('s', lambda e, tt0=t0, ww=w: e.copy(out=bc_sb[:, tt0:tt0 + ww],
                                                        in_=ps3[0:32, 0:ww]))
            pscons['ps3'] = ('s', cb2)
        c_bc = cnt['s']
        c_bc_spill = dma('g', 'g', d_bc[:, :], bc_sb[:, :], waits={'s': c_bc})
        ckpt('p4')

        cdw = 0
        for k in range(ET):
            cdw = dma('sy', 'w', wf32[:, 0:128], dtproj_w[:, k * 128:(k + 1) * 128],
                      waits={'v': cnt['v'], 's': cnt['s']} if k == 0 else {'s': cnt['s']})
            E('s', lambda e, kk=k: e.copy(out=wb16[:, kk * 128:(kk + 1) * 128],
                                          in_=wf32[:, 0:128]), waits={'w': cdw})
        c_dtw = cnt['s']
        d_spill = c_bc_spill
        spcons = [0, 0]
        for i in range(ET):
            for tgi, (t0, w) in enumerate(tgs):
                idx = i * 4 + tgi
                pn = 'ps0' if idx % 2 == 0 else 'ps1'
                pb = psbuf[pn]
                sp = SP0 if idx % 2 == 0 else SP1
                ct = E('t', lambda e, ii=i, tt0=t0, ww=w, p=pb: e.matmul(
                    out=p[:, 0:ww], lhsT=wb16[:, ii * 128:(ii + 1) * 128],
                    rhs=dt_sb[:, tt0:tt0 + ww], start=True, stop=True),
                    waits=mkw({'s': max(c_dtw, c_bc)}, pscons[pn]))
                ce = E('s', lambda e, ii=i, ww=w, p=pb, spo=sp: e.activation(
                    out=A_(spo, TG)[:, 0:ww], in_=p[:, 0:ww], func=AF.Exp,
                    bias=biases[:, 32 + ii:33 + ii]),
                    waits={'t': ct, 's': spcons[idx % 2]})
                pscons[pn] = ('s', ce)
                cp1 = E('v', lambda e, ww=w, spo=sp: e.tensor_scalar(
                    out=A_(spo, TG)[:, 0:ww], in0=A_(spo, TG)[:, 0:ww],
                    scalar1=1.0, scalar2=None, op0=OP.add), waits={'s': ce})
                cln = E('s', lambda e, ii=i, tt0=t0, ww=w, spo=sp: e.activation(
                    out=workb[:, tt0:tt0 + ww], in_=A_(spo, TG)[:, 0:ww], func=AF.Ln),
                    waits={'v': cp1, 'g': d_spill})
                spcons[idx % 2] = cln
            d_spill = dma('g', 'g', d_delta[i * 128:(i + 1) * 128, :], workb[:, 0:TOK],
                          waits={'s': cln})
            cst = E('v', lambda e: e.reduce_sum(out=stats[:, 0:1], in_=workb[:, 0:TOK],
                                                axis=AX), waits={'s': cln})
            E('s', lambda e, ii=i: e.activation(
                out=exch[:, ii * N:(ii + 1) * N], in_=Atile[:, ii * N:(ii + 1) * N],
                func=AF.Exp, scale=stats[:, 0:1]), waits={'v': max(cst, c_A)})
        c_p5s = cnt['s']; c_p5g = cnt['g']
        ckpt('p5')

        for nn in range(N):
            dma('sy', 'bc', bb7a(nn), d_bc[nn:nn + 1, :].to_broadcast([128, TOK]),
                waits={'g': c_bc_spill, 'v': cnt['v'], 's': cnt['s']} if nn == 0
                else None)
        c_bbl = cnt['bc']
        dacons = [0, 0]
        for i in range(ET):
            cld = dma('sy', 'ld', workb[:, P7_DLT:P7_DLT + TOK],
                      d_delta[i * 128:(i + 1) * 128, :],
                      waits={'g': c_p5g, 'v': cnt['v'], 's': cnt['s']})
            cld = dma('sy', 'ld', workb[:, P7_U:P7_U + TOK], d_u[i * 128:(i + 1) * 128, :])
            cdu = E('v', lambda e: e.tensor_tensor(out=workb[:, P7_DU:P7_DU + TOK],
                                                   in0=workb[:, P7_DLT:P7_DLT + TOK],
                                                   in1=workb[:, P7_U:P7_U + TOK],
                                                   op=OP.mult), waits={'ld': cld})
            for nn in range(N):
                da = DA0 if nn % 2 == 0 else DA1
                cda = E('s', lambda e, ii=i, r=nn, dao=da: e.activation(
                    out=A_(dao, TOK)[:, :], in_=workb[:, P7_DLT:P7_DLT + TOK],
                    func=AF.Exp, scale=Atile[:, ii * N + r:ii * N + r + 1]),
                    waits={'ld': cld, 'v': dacons[nn % 2]})
                cw3 = E('v', lambda e, r=nn: e.tensor_tensor(
                    out=workb[:, P7_W:P7_W + TOK], in0=workb[:, P7_DU:P7_DU + TOK],
                    in1=bb7a(r), op=OP.mult),
                    waits={'s': cda, 'bc': c_bbl, 'v': cdu})
                chh = E('v', lambda e, dao=da: e.tensor_tensor_scan(
                    out=A_(HF, TOK)[:, :], data0=A_(dao, TOK)[:, :],
                    data1=workb[:, P7_W:P7_W + TOK], initial=consts[:, 2:3],
                    op0=OP.mult, op1=OP.add))
                dacons[nn % 2] = chh
                E('v', lambda e, ii=i, r=nn: e.tensor_copy(
                    out=exch[:, 256 + ii * N + r:256 + ii * N + r + 1],
                    in_=A_(HF, TOK)[:, TOK - 1:TOK]), waits={'v': chh})
        c_scan_v = cnt['v']
        ckpt('p7a')

        cex = dma('g', 'g', d_exin[:, :], exch[:, :], waits={'v': c_scan_v, 's': c_p5s})
        ccc = E('g', lambda e: e.collective_compute(
            "AllGather", OP.bypass, replica_groups=[list(range(NCORES))],
            ins=[d_exin], outs=[d_exout]), inc=1, waits={'g': cex}, semn='cc')
        ckpt('cc')

        cgw = 0
        for k in range(DT):
            cgw = dma('sy', 'w', wf32[:, k * 8:(k + 1) * 8],
                      gate_wT[k * 128:(k + 1) * 128, :],
                      waits={'s': cnt['s'], 'v': cnt['v']} if k == 0 else None)
        cgs = E('s', lambda e: e.copy(out=wb16[:, G_GATE:G_GATE + 64], in_=wf32[:, 0:64]),
                waits={'w': cgw})
        c_psl = ('v', 0)
        c_pst = ('s', 0)
        for i in range(TT):
            for k in range(DT):
                fw = {'s': cgs, c_psl[0]: c_psl[1]} if k == 0 else None
                ct = E('t', lambda e, ii=i, kk=k: e.matmul(
                    out=psl[:, :], lhsT=xn(kk, ii * 128, 128),
                    rhs=wb16[:, G_GATE + kk * 8:G_GATE + (kk + 1) * 8],
                    start=(kk == 0), stop=(kk == DT - 1)),
                    inc=(1 if k == DT - 1 else 0), waits=fw)
            E('v', lambda e: e.tensor_tensor(out=A_(0, 8)[:, 0:8], in0=psl[:, :],
                                             in1=lbb[:, :], op=OP.add),
              waits={'t': ct, 's': cnt['s']})
            E('v', lambda e: e.reduce_max(out=stats[:, 0:1], in_=A_(0, 8)[:, 0:8], axis=AX),
              waits={'v': cnt['v']})
            E('v', lambda e: e.tensor_scalar(out=A_(8, 8)[:, 0:8], in0=A_(0, 8)[:, 0:8],
                                             scalar1=stats[:, 0:1], scalar2=None,
                                             op0=OP.is_equal), waits={'v': cnt['v']})
            E('v', lambda e: e.scalar_tensor_tensor(out=A_(0, 8)[:, 0:8], in0=A_(8, 8)[:, 0:8],
                                                    scalar=-1e30, in1=A_(0, 8)[:, 0:8],
                                                    op0=OP.mult, op1=OP.add),
              waits={'v': cnt['v']})
            E('v', lambda e: e.reduce_max(out=stats[:, 1:2], in_=A_(0, 8)[:, 0:8], axis=AX),
              waits={'v': cnt['v']})
            E('v', lambda e: e.tensor_scalar(out=A_(16, 8)[:, 0:8], in0=A_(0, 8)[:, 0:8],
                                             scalar1=stats[:, 1:2], scalar2=None,
                                             op0=OP.is_equal), waits={'v': cnt['v']})
            E('v', lambda e: e.tensor_tensor(out=A_(24, 8)[:, 0:8], in0=A_(8, 8)[:, 0:8],
                                             in1=psl[:, :], op=OP.mult), waits={'v': cnt['v']})
            E('v', lambda e: e.reduce_sum(out=stats[:, 2:3], in_=A_(24, 8)[:, 0:8], axis=AX),
              waits={'v': cnt['v']})
            E('v', lambda e: e.tensor_tensor(out=A_(24, 8)[:, 0:8], in0=A_(16, 8)[:, 0:8],
                                             in1=psl[:, :], op=OP.mult), waits={'v': cnt['v']})
            cpl = E('v', lambda e: e.reduce_sum(out=stats[:, 3:4], in_=A_(24, 8)[:, 0:8],
                                                axis=AX), waits={'v': cnt['v']})
            c_psl = ('v', cpl)
            cs4 = E('s', lambda e: e.activation(out=stats[:, 4:5], in_=stats[:, 2:3],
                                                func=AF.Sigmoid), waits={'v': cnt['v']})
            cs4 = E('s', lambda e: e.activation(out=stats[:, 5:6], in_=stats[:, 3:4],
                                                func=AF.Sigmoid), waits={'s': cnt['s']})
            E('v', lambda e: e.tensor_tensor(out=stats[:, 6:7], in0=stats[:, 4:5],
                                             in1=stats[:, 5:6], op=OP.add),
              waits={'s': cs4})
            E('v', lambda e: e.tensor_scalar(out=stats[:, 6:7], in0=stats[:, 6:7],
                                             scalar1=1e-6, scalar2=None, op0=OP.add),
              waits={'v': cnt['v']})
            E('v', lambda e: e.reciprocal(out=stats[:, 6:7], in_=stats[:, 6:7]),
              waits={'v': cnt['v']})
            E('v', lambda e: e.tensor_tensor(out=stats[:, 4:5], in0=stats[:, 4:5],
                                             in1=stats[:, 6:7], op=OP.mult),
              waits={'v': cnt['v']})
            E('v', lambda e: e.tensor_tensor(out=stats[:, 5:6], in0=stats[:, 5:6],
                                             in1=stats[:, 6:7], op=OP.mult),
              waits={'v': cnt['v']})
            E('v', lambda e: e.tensor_scalar(out=A_(24, 8)[:, 0:8], in0=A_(16, 8)[:, 0:8],
                                             scalar1=stats[:, 5:6], scalar2=None,
                                             op0=OP.mult), waits={'v': cnt['v']})
            cwe = E('v', lambda e, ii=i: e.scalar_tensor_tensor(
                out=we_sb[:, ii * 8:(ii + 1) * 8], in0=A_(8, 8)[:, 0:8],
                scalar=stats[:, 4:5], in1=A_(24, 8)[:, 0:8], op0=OP.mult, op1=OP.add),
                waits={'v': cnt['v']})
            ct3 = E('t', lambda e, ii=i: e.transpose(
                out=pst[0:8, 0:128], in_=we_sb[:, ii * 8:(ii + 1) * 8],
                identity=identf[:, :]), waits={'v': cwe, c_pst[0]: c_pst[1]})
            cpt = E('s', lambda e, ii=i: e.copy(out=weT[0:8, ii * 128:(ii + 1) * 128],
                                                in_=pst[0:8, 0:128]), waits={'t': ct3})
            c_pst = ('s', cpt)
        c_we = cnt['s']
        c_we_spill = dma('g', 'g', d_we[:, :], weT[0:8, :], waits={'s': c_we})

        cz = E('v', lambda e: e.memset(arena_b[:, MOE0:MOE0 + DT * TOK], 0.0),
               waits={'v': c_scan_v})

        c_w1mm = 0
        c_w2mm = 0
        c_web = 0
        for ex in range(NEXP + 1):
            cwl = 0
            for k in range(DT):
                src1 = w1T[ex, k * 128:(k + 1) * 128, :] if ex < NEXP \
                    else cfcT[k * 128:(k + 1) * 128, :]
                cwl = dma('sy', 'w', wf32[:, :], src1,
                          waits={'s': cnt['s'], 't': c_w1mm} if k == 0 else {'s': cnt['s']})
                E('s', lambda e, kk=k: e.copy(
                    out=workb[:, W1_0 + kk * 1024:W1_0 + (kk + 1) * 1024],
                    in_=wf32[:, :]), waits={'w': cwl})
            for k in range(HT):
                src2 = w2T[ex, k * 128:(k + 1) * 128, :] if ex < NEXP \
                    else cprojT[k * 128:(k + 1) * 128, :]
                cwl = dma('sy', 'w', wf32[:, :], src2,
                          waits={'s': cnt['s'], 't': c_w2mm} if k == 0 else {'s': cnt['s']})
                E('s', lambda e, kk=k: e.copy(
                    out=workb[:, W2_0 + kk * 1024:W2_0 + (kk + 1) * 1024],
                    in_=wf32[:, :]), waits={'w': cwl})
            c_wst = cnt['s']
            if ex < NEXP:
                c_web = dma('sy', 'bc', web[:, :],
                            d_we[ex:ex + 1, :].to_broadcast([128, TOK]),
                            waits={'g': c_we_spill, 'v': cnt['v'], 's': cnt['s']})
            for tgi, (t0, w) in enumerate(tgs):
                c_hid = 0
                for h in range(HT):
                    pn = 'ps0' if h % 2 == 0 else 'ps1'
                    pb = psbuf[pn]
                    for k in range(DT):
                        fw = mkw({'s': c_wst, 't': c_w2mm}, pscons[pn]) if k == 0 else None
                        c_w1mm = E('t', lambda e, kk=k, hh=h, tt0=t0, ww=w, p=pb: e.matmul(
                            out=p[:, 0:ww],
                            lhsT=workb[:, W1_0 + kk * 1024 + hh * 128:
                                       W1_0 + kk * 1024 + (hh + 1) * 128],
                            rhs=xn(kk, tt0, ww), start=(kk == 0), stop=(kk == DT - 1)),
                            inc=(1 if k == DT - 1 else 0), waits=fw)
                    cr = E('s', lambda e, p=pb, ww=w: e.activation(
                        out=wb16[:, G_RELU:G_RELU + ww], in_=p[:, 0:ww], func=AF.Relu),
                        waits={'t': c_w1mm})
                    pscons[pn] = ('s', cr)
                    if ex < NEXP:
                        cr = E('s', lambda e, ww=w: e.square(out=wb16[:, G_SQ:G_SQ + ww],
                                                             in_=wb16[:, G_RELU:G_RELU + ww]))
                        c_hid = E('v', lambda e, hh=h, tt0=t0, ww=w: e.tensor_tensor(
                            out=wb16[:, G_HID + hh * TG:G_HID + hh * TG + ww],
                            in0=wb16[:, G_SQ:G_SQ + ww], in1=web[:, tt0:tt0 + ww],
                            op=OP.mult), waits={'s': cr, 'bc': c_web, 't': c_w2mm})
                    else:
                        cr = E('s', lambda e, hh=h, ww=w: e.square(
                            out=wb16[:, G_HID + hh * TG:G_HID + hh * TG + ww],
                            in_=wb16[:, G_RELU:G_RELU + ww]))
                        c_hid = 0
                c_hs = cnt['s']
                for dd in range(DT):
                    pn = 'ps2' if dd % 2 == 0 else 'ps3'
                    pb = psbuf[pn]
                    for h in range(HT):
                        fw = mkw({'v': max(c_hid, cz), 's': c_hs}, pscons[pn]) \
                            if h == 0 else None
                        c_w2mm = E('t', lambda e, hh=h, d2=dd, ww=w, p=pb: e.matmul(
                            out=p[:, 0:ww],
                            lhsT=workb[:, W2_0 + hh * 1024 + d2 * 128:
                                       W2_0 + hh * 1024 + (d2 + 1) * 128],
                            rhs=wb16[:, G_HID + hh * TG:G_HID + hh * TG + ww],
                            start=(hh == 0), stop=(hh == HT - 1)),
                            inc=(1 if h == HT - 1 else 0), waits=fw)
                    ca = E('v', lambda e, d2=dd, tt0=t0, ww=w, p=pb: e.tensor_tensor(
                        out=moe(d2, tt0, ww), in0=moe(d2, tt0, ww), in1=p[:, 0:ww],
                        op=OP.add), waits={'t': c_w2mm, 'v': cz})
                    pscons[pn] = ('v', ca)
        for dd in range(DT):
            E('v', lambda e, d2=dd: e.tensor_scalar(
                out=moe(d2, 0, TOK), in0=moe(d2, 0, TOK),
                scalar1=biases[:, 48 + d2:49 + d2], scalar2=None, op0=OP.add))
        c_moe = cnt['v']
        ckpt('p6')

        cvh = E('v', lambda e: e.memset(hcur[:, :], 0.0), waits={'cc': 1})
        cvh = E('v', lambda e: e.memset(hin[:, :], 0.0))
        for j in range(NCORES):
            cldx = dma('sy', 'ex', wf32[:, 0:512], d_exout[j, :, :],
                       waits={'cc': 1, 'v': cnt['v'], 's': cnt['s']})
            cvh = E('v', lambda e, jj=j: e.scalar_tensor_tensor(
                out=hin[:, :], in0=hcur[:, :], scalar=maskb[:, jj:jj + 1], in1=hin[:, :],
                op0=OP.mult, op1=OP.add), waits={'ex': cldx, 'v': cnt['v']})
            cvh = E('v', lambda e: e.tensor_tensor(out=hcur[:, :], in0=hcur[:, :],
                                                   in1=wf32[:, 0:256], op=OP.mult),
                    waits={'v': cnt['v']})
            cvh = E('v', lambda e: e.tensor_tensor(out=hcur[:, :], in0=hcur[:, :],
                                                   in1=wf32[:, 256:512], op=OP.add),
                    waits={'v': cnt['v']})
        c_hin = cvh
        ckpt('comb')

        cww = 0
        for k in range(ET):
            cww = dma('sy', 'w', wf32[:, :], W_out[k * 128:(k + 1) * 128, :],
                      waits={'v': cnt['v'], 's': cnt['s'], 't': cnt['t']} if k == 0
                      else {'s': cnt['s']})
            E('s', lambda e, kk=k: e.copy(out=workb[:, kk * 1024:(kk + 1) * 1024],
                                          in_=wf32[:, :]), waits={'w': cww})
        c_wo = cnt['s']

        da7cons = [0, 0]
        p9p = [0, 0]
        c_outd = 0
        for tgi, (t0, w) in enumerate(tgs):
            cbl = 0
            for nn in range(N):
                fw = {'t': cnt['t'], 'v': cnt['v'], 's': cnt['s']} if (nn == 0) else None
                dma('sy', 'bc', bbt(nn, w),
                    d_bc[nn:nn + 1, t0:t0 + w].to_broadcast([128, w]), waits=fw)
                cbl = dma('sy', 'bc', cbt(nn, w),
                          d_bc[16 + nn:17 + nn, t0:t0 + w].to_broadcast([128, w]))
            for i in range(ET):
                cld = dma('sy', 'ld', wb16[:, B7_D:B7_D + w],
                          d_delta[i * 128:(i + 1) * 128, t0:t0 + w],
                          waits={'v': cnt['v'], 's': cnt['s']})
                cld = dma('sy', 'ld', wb16[:, B7_U:B7_U + w],
                          d_u[i * 128:(i + 1) * 128, t0:t0 + w])
                cld = dma('sy', 'ld', wb16[:, B7_XR:B7_XR + w],
                          d_xr[i * 128:(i + 1) * 128, t0:t0 + w])
                cdu = E('v', lambda e, ww=w: e.tensor_tensor(
                    out=wb16[:, B7_DU:B7_DU + ww], in0=wb16[:, B7_D:B7_D + ww],
                    in1=wb16[:, B7_U:B7_U + ww], op=OP.mult),
                    waits={'ld': cld, 'v': c_hin})
                for nn in range(N):
                    da = DA70 if nn % 2 == 0 else DA71
                    cda = E('s', lambda e, ii=i, r=nn, dao=da, ww=w: e.activation(
                        out=A_(dao, TG)[:, 0:ww], in_=wb16[:, B7_D:B7_D + ww],
                        func=AF.Exp, scale=Atile[:, ii * N + r:ii * N + r + 1]),
                        waits={'ld': cld, 'v': da7cons[nn % 2]})
                    E('v', lambda e, r=nn, ww=w: e.tensor_tensor(
                        out=wb16[:, B7_W:B7_W + ww], in0=wb16[:, B7_DU:B7_DU + ww],
                        in1=bbt(r, ww), op=OP.mult),
                        waits={'s': cda, 'bc': cbl, 'v': cdu})
                    chh = E('v', lambda e, ii=i, r=nn, dao=da, ww=w: e.tensor_tensor_scan(
                        out=A_(HF7, TG)[:, 0:ww], data0=A_(dao, TG)[:, 0:ww],
                        data1=wb16[:, B7_W:B7_W + ww],
                        initial=hin[:, ii * N + r:ii * N + r + 1],
                        op0=OP.mult, op1=OP.add))
                    da7cons[nn % 2] = chh
                    E('v', lambda e, ii=i, r=nn, ww=w: e.tensor_copy(
                        out=hin[:, ii * N + r:ii * N + r + 1],
                        in_=A_(HF7, TG)[:, ww - 1:ww]), waits={'v': chh})
                    if nn == 0:
                        cy = E('v', lambda e, r=nn, ww=w: e.tensor_tensor(
                            out=A_(Y7, TG)[:, 0:ww], in0=A_(HF7, TG)[:, 0:ww],
                            in1=cbt(r, ww), op=OP.mult))
                    else:
                        E('v', lambda e, r=nn, ww=w: e.tensor_tensor(
                            out=A_(HC7, TG)[:, 0:ww], in0=A_(HF7, TG)[:, 0:ww],
                            in1=cbt(r, ww), op=OP.mult))
                        cy = E('v', lambda e, ww=w: e.tensor_tensor(
                            out=A_(Y7, TG)[:, 0:ww], in0=A_(Y7, TG)[:, 0:ww],
                            in1=A_(HC7, TG)[:, 0:ww], op=OP.add))
                cy = E('v', lambda e, ii=i, ww=w: e.scalar_tensor_tensor(
                    out=A_(Y7, TG)[:, 0:ww], in0=wb16[:, B7_U:B7_U + ww],
                    scalar=biases[:, 56 + ii:57 + ii], in1=A_(Y7, TG)[:, 0:ww],
                    op0=OP.mult, op1=OP.add))
                cy = E('v', lambda e, ii=i, ww=w: e.tensor_tensor(
                    out=yxt(ii, ww), in0=A_(Y7, TG)[:, 0:ww],
                    in1=wb16[:, B7_XR:B7_XR + ww], op=OP.mult))
            c_yx = cnt['v']
            for dd in range(DT):
                pn = 'ps0' if dd % 2 == 0 else 'ps1'
                pb = psbuf[pn]
                for k in range(ET):
                    fw = mkw({'s': c_wo, 'v': c_yx}, pscons[pn]) if k == 0 else None
                    ct = E('t', lambda e, kk=k, d2=dd, ww=w, p=pb: e.matmul(
                        out=p[:, 0:ww],
                        lhsT=workb[:, kk * 1024 + d2 * 128:kk * 1024 + (d2 + 1) * 128],
                        rhs=yxt(kk, ww), start=(kk == 0), stop=(kk == ET - 1)),
                        inc=(1 if k == ET - 1 else 0), waits=fw)
                ca = E('v', lambda e, d2=dd, tt0=t0, ww=w, p=pb: e.tensor_tensor(
                    out=moe(d2, tt0, ww), in0=moe(d2, tt0, ww), in1=p[:, 0:ww],
                    op=OP.add), waits={'t': ct, 'v': c_moe})
                pscons[pn] = ('v', ca)
            c_mix = cnt['v']
            for tt4 in range(4):
                ti = tgi * 4 + tt4
                cx2 = dma('sy', 'x', wf32[:, :], x_in[3 + ti * 128:3 + (ti + 1) * 128, :],
                          waits={'v': cnt['v'], 's': cnt['s']})
                for dd in range(DT):
                    ct = E('t', lambda e, d2=dd, o=tt4 * 128: e.transpose(
                        out=pstb[:, d2 * 128:(d2 + 1) * 128], in_=moe(d2, t0 + o, 128),
                        identity=identb[:, :]),
                        inc=(1 if dd == DT - 1 else 0),
                        waits={'v': max(c_mix, p9p[0]), 's': c_wo} if dd == 0 else None)
                cadd = E('v', lambda e: e.tensor_tensor(
                    out=A_(OUT7, 1024)[:, :], in0=pstb[:, :], in1=wf32[:, :],
                    op=OP.add), waits={'t': ct, 'x': cx2, 'g': c_outd})
                p9p[0] = cadd
                c_outd = dma('g', 'g', out[ti * 128:(ti + 1) * 128, :],
                             A_(OUT7, 1024)[:, :], waits={'v': cnt['v']})

    with nc.Block() as block:
        @block.sync
        def _(sync):
            make_walk('sy', sync)

        @block.vector
        def _(vector):
            make_walk('v', vector)

        @block.scalar
        def _(scalar):
            make_walk('s', scalar)

        @block.tensor
        def _(tensor):
            make_walk('t', tensor)

        @block.gpsimd
        def _(gpsimd):
            make_walk('g', gpsimd)

    for t in reversed(ctx):
        t.__exit__(None, None, None)
    return nc


_NC_CACHE = {}

def _get_nc():
    if 0 not in _NC_CACHE:
        with allow_ici_all_to_all():
            _NC_CACHE[0] = build_program()
    return _NC_CACHE[0]


def kernel(L0_attn_x, L0_attn_W_in, L0_attn_b_in, L0_attn_conv_w, L0_attn_conv_b,
           L0_attn_x_proj_w, L0_attn_dt_proj_w, L0_attn_dt_bias, L0_attn_A_log,
           L0_attn_D_param, L0_attn_W_out, L0_attn_b_out, L0_ffn_gate_w,
           L0_ffn_lb_bias, L0_ffn_w1, L0_ffn_w2, L0_ffn_cfc, L0_ffn_cproj):
    f32 = np.float32
    x = np.asarray(L0_attn_x, f32).reshape(L, D)
    com = dict(
        W_in=np.ascontiguousarray(np.asarray(L0_attn_W_in, f32)),
        b_in=np.ascontiguousarray(np.asarray(L0_attn_b_in, f32).reshape(1, 2 * EE)),
        conv_w=np.ascontiguousarray(np.asarray(L0_attn_conv_w, f32).reshape(EE, 4)),
        conv_b=np.ascontiguousarray(np.asarray(L0_attn_conv_b, f32).reshape(1, EE)),
        xproj_w=np.ascontiguousarray(np.asarray(L0_attn_x_proj_w, f32)),
        dtproj_w=np.ascontiguousarray(np.asarray(L0_attn_dt_proj_w, f32)),
        dt_bias=np.ascontiguousarray(np.asarray(L0_attn_dt_bias, f32).reshape(1, EE)),
        A_log=np.ascontiguousarray(np.asarray(L0_attn_A_log, f32)),
        D_par=np.ascontiguousarray(np.asarray(L0_attn_D_param, f32).reshape(1, EE)),
        W_out=np.ascontiguousarray(np.asarray(L0_attn_W_out, f32)),
        b_out=np.ascontiguousarray(np.asarray(L0_attn_b_out, f32).reshape(1, D)),
        gate_wT=np.ascontiguousarray(np.asarray(L0_ffn_gate_w, f32).T),
        lb_bias=np.ascontiguousarray(np.asarray(L0_ffn_lb_bias, f32).reshape(1, NEXP)),
        w1T=np.ascontiguousarray(np.asarray(L0_ffn_w1, f32).transpose(0, 2, 1)),
        w2T=np.ascontiguousarray(np.asarray(L0_ffn_w2, f32).transpose(0, 2, 1)),
        cfcT=np.ascontiguousarray(np.asarray(L0_ffn_cfc, f32).T),
        cprojT=np.ascontiguousarray(np.asarray(L0_ffn_cproj, f32).T),
        ident_in=np.eye(128, dtype=f32),
    )
    in_maps = []
    for k in range(NCORES):
        xs = np.empty((TOK + 3, D), f32)
        if k == 0:
            xs[0:3] = 0.0
        else:
            xs[0:3] = x[k * TOK - 3:k * TOK]
        xs[3:] = x[k * TOK:(k + 1) * TOK]
        m = dict(com)
        m['x_in'] = xs
        m['halo_mask'] = np.array([[0.0 if k == 0 else 1.0]], f32)
        m['core_mask'] = np.eye(NCORES, dtype=f32)[k:k + 1]
        in_maps.append(m)

    nc = _get_nc()
    res = run_bass_kernel_spmd(nc, in_maps, core_ids=list(range(NCORES)))
    outp = np.concatenate([r['out'] for r in res.results], axis=0).reshape(1, L, D)
    return outp.astype(np.float32)
